# version 01
# baseline (speedup 1.0000x reference)
import math
from contextlib import ExitStack

import numpy as np
import ml_dtypes

import concourse.bass as bass
import concourse.mybir as mybir
from concourse.bass_utils import run_bass_kernel_spmd

F32 = mybir.dt.float32
BF16 = mybir.dt.bfloat16
I32 = mybir.dt.int32
AF = mybir.ActivationFunctionType
ALU = mybir.AluOpType
AX = mybir.AxisListType
BF = ml_dtypes.bfloat16

NCORES = 8
B, SQ, D = 8, 2048, 2048
GINTER = 5504
QINTER = 16384
NEG = -30000.0


def cdiv(a, b):
    return (a + b - 1) // b


def row_tiles(n):
    out = []
    r = 0
    while r < n:
        p = min(128, n - r)
        out.append((r, p))
        r += p
    return out



def build_dequant_nc(shard):
    nc = bass.Bass("TRN2", target_bir_lowering=False)

    gg = nc.dram_tensor("gg", [shard["gg"], D], I32, kind="ExternalInput").ap()
    gu = nc.dram_tensor("gu", [shard["gu"], D], I32, kind="ExternalInput").ap()
    gd = nc.dram_tensor("gd", [shard["gd"], GINTER], I32, kind="ExternalInput").ap()
    lut = nc.dram_tensor("lut", [1, 16], F32, kind="ExternalInput").ap()
    qg = nc.dram_tensor("qg", [shard["qg"], D // 2], I32, kind="ExternalInput").ap()
    qgs = nc.dram_tensor("qgs", [shard["qg"], D // 32], F32, kind="ExternalInput").ap()
    qu = nc.dram_tensor("qu", [shard["qu"], D // 2], I32, kind="ExternalInput").ap()
    qus = nc.dram_tensor("qus", [shard["qu"], D // 32], F32, kind="ExternalInput").ap()
    qd = nc.dram_tensor("qd", [shard["qd"], QINTER // 2], I32, kind="ExternalInput").ap()
    qds = nc.dram_tensor("qds", [shard["qd"], QINTER // 32], F32, kind="ExternalInput").ap()

    wgg = nc.dram_tensor("wgg", [shard["gg"], D], BF16, kind="ExternalOutput").ap()
    wgu = nc.dram_tensor("wgu", [shard["gu"], D], BF16, kind="ExternalOutput").ap()
    wgd = nc.dram_tensor("wgd", [shard["gd"], GINTER], BF16, kind="ExternalOutput").ap()
    wqg = nc.dram_tensor("wqg", [shard["qg"], D], BF16, kind="ExternalOutput").ap()
    wqu = nc.dram_tensor("wqu", [shard["qu"], D], BF16, kind="ExternalOutput").ap()
    wqd = nc.dram_tensor("wqd", [shard["qd"], QINTER], BF16, kind="ExternalOutput").ap()

    CW = 1024

    items = []

    def add_matrix(kind, in_ap, sc_ap, out_ap, rows, cols):
        for (r0, p) in row_tiles(rows):
            c0 = 0
            while c0 < cols:
                cw = min(CW, cols - c0)
                item = dict(kind=kind, p=p, cw=cw)
                item["in"] = in_ap[r0:r0 + p, c0:c0 + cw]
                if kind == "nv":
                    sb0, sbn = c0 // 16, cw // 16
                    item["sc"] = sc_ap[r0:r0 + p, sb0:sb0 + sbn]
                    item["out"] = out_ap[r0:r0 + p, 2 * c0:2 * (c0 + cw)]
                else:
                    item["out"] = out_ap[r0:r0 + p, c0:c0 + cw]
                items.append(item)
                c0 += cw

    add_matrix("gh", gg, None, wgg, shard["gg"], D)
    add_matrix("gh", gu, None, wgu, shard["gu"], D)
    add_matrix("gh", gd, None, wgd, shard["gd"], GINTER)
    add_matrix("nv", qg, qgs, wqg, shard["qg"], D // 2)
    add_matrix("nv", qu, qus, wqu, shard["qu"], D // 2)
    add_matrix("nv", qd, qds, wqd, shard["qd"], QINTER // 2)

    with ExitStack() as ctx:
        def sb(name, shape, dt):
            return ctx.enter_context(nc.sbuf_tensor(name, shape, dt))
        inb = [sb(f"inb{i}", [128, CW], I32) for i in range(2)]
        scb = [sb(f"scb{i}", [128, CW // 16], F32) for i in range(2)]
        outb = [sb(f"outb{i}", [128, 2 * CW], BF16) for i in range(2)]
        ti = sb("ti", [128, CW], I32)
        t2 = sb("t2", [128, CW], I32)
        e_f = sb("e_f", [128, CW], F32)
        m_f = sb("m_f", [128, CW], F32)
        sgn = sb("sgn", [128, CW], F32)
        v_f = sb("v_f", [128, CW], F32)
        su_f = sb("su_f", [128, CW], F32)
        b0f = sb("b0f", [128, CW], F32)
        lv = sb("lv", [128, 8 * CW], F32)
        lutb = sb("lutb", [128, 16], F32)
        dlut = sb("dlut", [128, 8], F32)

        ld = ctx.enter_context(nc.semaphore("ld"))
        cp = ctx.enter_context(nc.semaphore("cp"))
        st = ctx.enter_context(nc.semaphore("st"))
        block = ctx.enter_context(nc.Block())

        cum = []
        tot = 16
        for it in items:
            tot += 16 if it["kind"] == "gh" else 32
            cum.append(tot)

        @block.sync
        def _(sy):
            sy.dma_start(out=lutb[:], in_=lut.broadcast_to((128, 16))).then_inc(ld, 16)
            for i, it in enumerate(items):
                if i >= 2:
                    sy.wait_ge(cp, i - 1)
                p, cw = it["p"], it["cw"]
                sy.dma_start(out=inb[i % 2][0:p, 0:cw], in_=it["in"]).then_inc(ld, 16)
                if it["kind"] == "nv":
                    sc = it["sc"]
                    sbn = sc.shape[1]
                    sy.dma_start(out=scb[i % 2][0:p, 0:sbn], in_=sc).then_inc(ld, 16)

        @block.vector
        def _(v):
            v.wait_ge(ld, 16)
            v.tensor_tensor(dlut[:], lutb[:, 1:16:2], lutb[:, 0:16:2], ALU.subtract)
            for i, it in enumerate(items):
                v.wait_ge(ld, cum[i])
                if i >= 2:
                    v.wait_ge(st, 16 * (i - 1))
                p, cw = it["p"], it["cw"]
                pk = inb[i % 2][0:p, 0:cw]
                out = outb[i % 2]
                if it["kind"] == "gh":
                    last = _emit_ghost(v, pk, out[0:p, 0:cw],
                                       ti[0:p, 0:cw], t2[0:p, 0:cw], b0f[0:p, 0:cw],
                                       lv, lutb, dlut, p, cw)
                else:
                    sc = scb[i % 2][0:p, 0:cw // 16]
                    for shift in (0, 4):
                        last = _emit_nv(v, pk, out[0:p, shift // 4:2 * cw:2], sc,
                                        ti[0:p, 0:cw], t2[0:p, 0:cw], e_f[0:p, 0:cw],
                                        m_f[0:p, 0:cw], sgn[0:p, 0:cw], v_f[0:p, 0:cw],
                                        su_f[0:p, 0:cw], shift)
                last.then_inc(cp, 1)

        @block.scalar
        def _(sc_e):
            for i, it in enumerate(items):
                sc_e.wait_ge(cp, i + 1)
                p, cw = it["p"], it["cw"]
                w = 2 * cw if it["kind"] == "nv" else cw
                sc_e.dma_start(out=it["out"], in_=outb[i % 2][0:p, 0:w]).then_inc(st, 16)
            sc_e.wait_ge(st, 16 * len(items))
    return nc, items


def _emit_ghost(v, idx, out, ti, t2, b0f, lv, lutb, dlut, p, cw):
    CWF = lv.shape[1] // 8
    lvs = [lv[0:p, k * CWF:k * CWF + cw] for k in range(8)]
    v.tensor_scalar(t2, idx, 1, None, ALU.bitwise_and)
    v.tensor_copy(b0f, t2)
    for k in range(8):
        v.tensor_scalar(lvs[k], b0f, dlut[0:p, k:k + 1],
                        lutb[0:p, 2 * k:2 * k + 1], ALU.mult, ALU.add)
    v.tensor_scalar(ti, idx, 1, 1, ALU.logical_shift_right, ALU.bitwise_and)
    for k in range(4):
        v.select(lvs[k], ti, lvs[2 * k + 1], lvs[2 * k])
    v.tensor_scalar(ti, idx, 2, 1, ALU.logical_shift_right, ALU.bitwise_and)
    for k in range(2):
        v.select(lvs[k], ti, lvs[2 * k + 1], lvs[2 * k])
    v.tensor_scalar(ti, idx, 3, 1, ALU.logical_shift_right, ALU.bitwise_and)
    v.select(lvs[0], ti, lvs[1], lvs[0])
    return v.tensor_copy(out, lvs[0])


def _emit_nv(v, pk, out_strided, scb, ti, t2, e_f, m_f, sgn, v_f, su_f, shift):
    if shift:
        v.tensor_scalar(ti, pk, shift, 15, ALU.logical_shift_right, ALU.bitwise_and)
    else:
        v.tensor_scalar(ti, pk, 15, None, ALU.bitwise_and)
    v.tensor_scalar(t2, ti, 1, 3, ALU.logical_shift_right, ALU.bitwise_and)
    v.tensor_copy(e_f, t2)
    v.tensor_scalar(t2, ti, 1, None, ALU.bitwise_and)
    v.tensor_copy(m_f, t2)
    v.tensor_scalar(t2, ti, 3, 1, ALU.logical_shift_right, ALU.bitwise_and)
    v.tensor_copy(sgn, t2)
    v.tensor_scalar(sgn, sgn, -2.0, 1.0, ALU.mult, ALU.add)
    v.tensor_tensor(v_f, e_f, e_f, ALU.mult)
    v.tensor_scalar(v_f, v_f, 5.0, None, ALU.add)
    v.tensor_tensor(v_f, v_f, e_f, ALU.mult)
    v.tensor_scalar(v_f, v_f, 6.0, 1.0 / 6.0, ALU.add, ALU.mult)
    v.tensor_scalar(su_f, m_f, 0.25, 0.5, ALU.mult, ALU.add)
    v.tensor_tensor(v_f, v_f, su_f, ALU.mult)
    v.tensor_scalar(su_f, m_f, 0.5, None, ALU.mult)
    v.tensor_scalar(t2, e_f, 0.0, None, ALU.is_equal)
    v.select(v_f, t2, su_f, v_f)
    v.tensor_tensor(v_f, v_f, sgn, ALU.mult)
    p, cw = v_f.shape
    sbn = cw // 16
    v.tensor_tensor(v_f.rearrange("p (b r) -> p b r", r=16),
                    v_f.rearrange("p (b r) -> p b r", r=16),
                    scb.unsqueeze(2).broadcast_to((p, sbn, 16)), ALU.mult)
    return v.tensor_copy(out_strided, v_f)


def run_dequant(inp):
    shard = dict(gg=GINTER // 8, gu=GINTER // 8, gd=D // 8,
                 qg=QINTER // 8, qu=QINTER // 8, qd=D // 8)
    nc, _ = build_dequant_nc(shard)
    in_maps = []
    for c in range(8):
        def sl(a, n):
            return np.ascontiguousarray(a[c * n:(c + 1) * n])
        in_maps.append({
            "gg": sl(inp["L0_ffn_gate_idx"], shard["gg"]),
            "gu": sl(inp["L0_ffn_up_idx"], shard["gu"]),
            "gd": sl(inp["L0_ffn_down_idx"], shard["gd"]),
            "lut": np.ascontiguousarray(inp["L0_ffn_lut"].reshape(1, 16)),
            "qg": sl(inp["L1_ffn_gate_W_packed"], shard["qg"]),
            "qgs": sl(inp["L1_ffn_gate_scales"], shard["qg"]),
            "qu": sl(inp["L1_ffn_up_W_packed"], shard["qu"]),
            "qus": sl(inp["L1_ffn_up_scales"], shard["qu"]),
            "qd": sl(inp["L1_ffn_down_W_packed"], shard["qd"]),
            "qds": sl(inp["L1_ffn_down_scales"], shard["qd"]),
        })
    res = run_bass_kernel_spmd(nc, in_maps, core_ids=list(range(8)))
    out = {}
    for k in ["wgg", "wgu", "wgd", "wqg", "wqu", "wqd"]:
        out[k] = np.concatenate([res.results[c][k] for c in range(8)], axis=0)
    return out



def build_resid_ln_nc(n_add, out_x, out_xn):
    nc = bass.Bass("TRN2", target_bir_lowering=False)
    x_t = nc.dram_tensor("x", [D, SQ], F32, kind="ExternalInput").ap()
    adds = [nc.dram_tensor(f"a{i}", [D, SQ], F32, kind="ExternalInput").ap()
            for i in range(n_add)]
    xo = nc.dram_tensor("xo", [D, SQ], F32, kind="ExternalOutput").ap() if out_x else None
    xn = nc.dram_tensor("xn", [D, SQ], BF16, kind="ExternalOutput").ap() if out_xn else None
    NCH = D // 128
    NT = SQ // 512

    with ExitStack() as ctx:
        def sb(name, shape, dt):
            return ctx.enter_context(nc.sbuf_tensor(name, shape, dt))
        xc = [sb(f"xc{i}", [128, SQ], F32) for i in range(NCH)]
        ab = [sb(f"ab{i}", [128, SQ], F32) for i in range(max(1, n_add))]
        acc = sb("acc", [128, SQ], F32)
        acq = sb("acq", [128, SQ], F32)
        sq = sb("sq", [128, SQ], F32)
        ones_col = sb("ones_col", [128, 1], F32)
        ones_row = sb("ones_row", [1, 128], F32)
        mean_r = sb("mean_r", [1, SQ], F32)
        rstd_r = sb("rstd_r", [1, SQ], F32)
        m_b = sb("m_b", [128, SQ], F32)
        r_b = sb("r_b", [128, SQ], F32)
        xnc = sb("xnc", [128, SQ], BF16)
        ps1 = ctx.enter_context(nc.psum_tensor("ps1", [1, 512], F32))
        ps2 = ctx.enter_context(nc.psum_tensor("ps2", [1, 512], F32))
        psm = ctx.enter_context(nc.psum_tensor("psm", [128, 512], F32))
        psr = ctx.enter_context(nc.psum_tensor("psr", [128, 512], F32))

        ld = ctx.enter_context(nc.semaphore("ld"))
        cp = ctx.enter_context(nc.semaphore("cp"))
        mm = ctx.enter_context(nc.semaphore("mm"))
        sc = ctx.enter_context(nc.semaphore("sc"))
        st = ctx.enter_context(nc.semaphore("st"))
        st2 = ctx.enter_context(nc.semaphore("st2"))
        block = ctx.enter_context(nc.Block())

        per = 16 * (1 + n_add)

        @block.sync
        def _(sy):
            for c in range(NCH):
                if n_add and c >= 1:
                    sy.wait_ge(cp, c)
                sy.dma_start(out=xc[c][:], in_=x_t[c * 128:(c + 1) * 128, :]).then_inc(ld, 16)
                for i in range(n_add):
                    sy.dma_start(out=ab[i][:], in_=adds[i][c * 128:(c + 1) * 128, :]).then_inc(ld, 16)

        @block.vector
        def _(v):
            v.memset(ones_col[:], 1.0)
            v.memset(ones_row[:], 1.0)
            for c in range(NCH):
                v.wait_ge(ld, per * (c + 1))
                for i in range(n_add):
                    ins = v.tensor_tensor(xc[c][:], xc[c][:], ab[i][:], ALU.add)
                    if i == n_add - 1:
                        ins.then_inc(cp, 1)
                if out_xn:
                    if c == 0:
                        v.tensor_copy(acc[:], xc[0][:])
                        v.tensor_tensor(acq[:], xc[0][:], xc[0][:], ALU.mult)
                    else:
                        v.tensor_tensor(acc[:], acc[:], xc[c][:], ALU.add)
                        v.tensor_tensor(sq[:], xc[c][:], xc[c][:], ALU.mult)
                        v.tensor_tensor(acq[:], acq[:], sq[:], ALU.add)
            if not n_add:
                v.engine_nop().then_inc(cp, NCH)
            if not out_xn:
                return
            v.engine_nop().then_inc(cp, 1)
            for tt in range(NT):
                v.wait_ge(mm, tt + 1)
                sl = slice(tt * 512, (tt + 1) * 512)
                v.tensor_scalar(mean_r[0:1, sl], ps1[:, :], 1.0 / D, None, ALU.mult)
                v.tensor_scalar(rstd_r[0:1, sl], ps2[:, :], 1.0 / D, None, ALU.mult)
                v.tensor_tensor(sq[0:1, sl], mean_r[0:1, sl], mean_r[0:1, sl], ALU.mult)
                v.tensor_tensor(rstd_r[0:1, sl], rstd_r[0:1, sl], sq[0:1, sl],
                                ALU.subtract)
                v.tensor_scalar(rstd_r[0:1, sl], rstd_r[0:1, sl], 1e-5, None,
                                ALU.add).then_inc(cp, 1)
            v.wait_ge(sc, 1)
            v.reciprocal(rstd_r[0:1, :], rstd_r[0:1, :]).then_inc(cp, 1)
            for tt in range(NT):
                v.wait_ge(mm, NT + 2 * (tt + 1))
                sl = slice(tt * 512, (tt + 1) * 512)
                v.tensor_copy(m_b[:, sl], psm[:, :])
                ins = v.tensor_copy(r_b[:, sl], psr[:, :])
                ins.then_inc(cp, 1)
            for c in range(NCH):
                v.tensor_tensor(sq[:], xc[c][:], m_b[:], ALU.subtract)
                v.tensor_tensor(xnc[:], sq[:], r_b[:], ALU.mult).then_inc(cp, 1)
                v.wait_ge(st2, 16 * (c + 1))

        @block.tensor
        def _(t):
            if not out_xn:
                return
            t.wait_ge(cp, NCH + 1)
            for tt in range(NT):
                sl = slice(tt * 512, (tt + 1) * 512)
                if tt:
                    t.wait_ge(cp, NCH + 1 + tt)
                t.matmul(ps1[:, :], ones_col[:], acc[:, sl], start=True, stop=True)
                t.matmul(ps2[:, :], ones_col[:], acq[:, sl],
                         start=True, stop=True).then_inc(mm, 1)
            t.wait_ge(cp, NCH + NT + 2)
            for tt in range(NT):
                sl = slice(tt * 512, (tt + 1) * 512)
                if tt:
                    t.wait_ge(cp, NCH + NT + 2 + tt)
                t.matmul(psm[:, :], ones_row[:], mean_r[0:1, sl], start=True, stop=True)
                t.matmul(psr[:, :], ones_row[:], rstd_r[0:1, sl],
                         start=True, stop=True).then_inc(mm, 2)

        @block.scalar
        def _(se):
            if out_xn:
                se.wait_ge(cp, NCH + NT + 1)
                se.activation(rstd_r[0:1, :], rstd_r[0:1, :],
                              AF.Sqrt).then_inc(sc, 1)
            nst = 0
            if out_x:
                for c in range(NCH):
                    se.wait_ge(cp, c + 1)
                    se.dma_start(out=xo[c * 128:(c + 1) * 128, :],
                                 in_=xc[c][:]).then_inc(st, 16)
                    nst += 16

        @block.gpsimd
        def _(g):
            nst = 0
            if out_xn:
                base = NCH + NT + 2 + NT
                for c in range(NCH):
                    g.wait_ge(cp, base + c + 1)
                    g.dma_start(out=xn[c * 128:(c + 1) * 128, :],
                                in_=xnc[:]).then_inc(st2, 16)

        @block.sync
        def _(sy):
            if out_x:
                sy.wait_ge(st, 16 * NCH)
            if out_xn:
                sy.wait_ge(st2, 16 * NCH)
    return nc


def run_resid_ln(xs, adds_list, out_x, out_xn):
    n_add = len(adds_list)
    nc = build_resid_ln_nc(n_add, out_x, out_xn)
    in_maps = []
    for c in range(8):
        m = {"x": xs[c]}
        for i, al in enumerate(adds_list):
            m[f"a{i}"] = al[c]
        in_maps.append(m)
    res = run_bass_kernel_spmd(nc, in_maps, core_ids=list(range(8)))
    xo = [res.results[c].get("xo") for c in range(8)] if out_x else None
    xn = [res.results[c].get("xn") for c in range(8)] if out_xn else None
    return xo, xn



def build_ffn_nc(OC, G, TB, act_fn):
    NOG = OC // G
    NTB = SQ // TB
    NTT = TB // 512
    HC = D // 128
    multi = NOG > 1

    nc = bass.Bass("TRN2", target_bir_lowering=False)
    xn = nc.dram_tensor("xn", [D, SQ], BF16, kind="ExternalInput").ap()
    wg_s = nc.dram_tensor("wg_s", [OC, 128, D], BF16, kind="ExternalInput").ap()
    wu_s = nc.dram_tensor("wu_s", [OC, 128, D], BF16, kind="ExternalInput").ap()
    wd_s = nc.dram_tensor("wd_s", [NOG, HC, 128, G * 128], BF16,
                          kind="ExternalInput").ap()
    out = nc.dram_tensor("out", [D, SQ], F32, kind="ExternalOutput").ap()

    sched = {k: [] for k in ["sync", "tensor", "scalar", "vector", "gpsimd"]}
    cnt = {}

    with ExitStack() as ctx:
        def sb(name, shape, dt):
            return ctx.enter_context(nc.sbuf_tensor(name, shape, dt))
        xnb = [sb(f"xnb{h}", [128, TB], BF16) for h in range(HC)]
        wgb = [sb(f"wgb{i}", [128, D], BF16) for i in range(2)]
        wub = [sb(f"wub{i}", [128, D], BF16) for i in range(2)]
        wdb = [sb(f"wdb{i}", [128, G * 128], BF16) for i in range(2)]
        g_t = [sb(f"g_t{i}", [128, 512], BF16) for i in range(2)]
        u_t = [sb(f"u_t{i}", [128, 512], BF16) for i in range(2)]
        inter = [sb(f"inter{i}", [128, TB], BF16) for i in range(G)]
        if multi:
            accb = [sb(f"accb{i}", [128, TB], F32) for i in range(HC)]
        else:
            outb = [sb(f"outb{i}", [128, TB], F32) for i in range(2)]
        psg = [ctx.enter_context(nc.psum_tensor(f"psg{i}", [128, 512], F32))
               for i in range(2)]
        psu = [ctx.enter_context(nc.psum_tensor(f"psu{i}", [128, 512], F32))
               for i in range(2)]
        psd = [ctx.enter_context(nc.psum_tensor(f"psd{i}", [128, 512], F32))
               for i in range(2)]
        semh = {n: ctx.enter_context(nc.semaphore(n))
                for n in ["ld", "mgu", "md", "egu", "xin", "aup", "st"]}
        block = ctx.enter_context(nc.Block())

        def W(eng, sem, val):
            if val > 0:
                sched[eng].append(lambda e, s=semh[sem], v=val: e.wait_ge(s, v))

        def bump(sem, n=1):
            cnt[sem] = cnt.get(sem, 0) + n
            return cnt[sem]

        def DMA(eng, dst, src, sem="ld"):
            v = bump(sem, 16)
            sched[eng].append(
                lambda e, d=dst, sr=src, sm=semh[sem]:
                e.dma_start(out=d, in_=sr).then_inc(sm, 16))
            return v

        def MM(out_ap, l_ap, r_ap, st_, sp_, sem=None):
            v = bump(sem) if sem else None
            sched["tensor"].append(
                lambda e, o=out_ap, l=l_ap, r=r_ap, a=st_, b=sp_,
                sm=(semh[sem] if sem else None):
                (e.matmul(o, l, r, start=a, stop=b).then_inc(sm, 1) if sm else
                 e.matmul(o, l, r, start=a, stop=b)))
            return v

        def OP(eng, fn, sem=None):
            v = bump(sem) if sem else None
            sm = semh[sem] if sem else None
            sched[eng].append(
                lambda e, f=fn, s=sm: (f(e).then_inc(s, 1) if s else f(e)))
            return v

        mgu_done = {}
        egu_done = {}
        xin_done = {}
        md_done = {}
        aup_done = {}
        st_done = {}
        ld_w = {}

        for tb in range(NTB):
            tsl = slice(tb * TB, (tb + 1) * TB)
            if tb:
                W("sync", "mgu", mgu_done[(tb - 1, OC - 1, NTT - 1, "u")])
            for h in range(HC):
                ld_xn = DMA("sync", xnb[h][:, :], xn[h * 128:(h + 1) * 128, tsl])
            for og in range(NOG):
                for ol in range(G):
                    oc = og * G + ol
                    par = oc % 2
                    if oc >= 2:
                        W("sync", "mgu", mgu_done[(tb, oc - 1, NTT - 1, "u")])
                    DMA("sync", wgb[par][:], wg_s[oc])
                    ld_w[(tb, oc)] = DMA("sync", wub[par][:], wu_s[oc])
                    W("tensor", "ld", max(ld_w[(tb, oc)], ld_xn))
                    for tt in range(NTT):
                        use = (tb * OC + oc) * NTT + tt
                        pp = use % 2
                        if use >= 2:
                            W("tensor", "egu", egu_done[use - 2])
                        for h in range(HC):
                            MM(psg[pp][:, :], wgb[par][:, h * 128:(h + 1) * 128],
                               xnb[h][:, tt * 512:(tt + 1) * 512], h == 0, h == HC - 1,
                               sem=None if h < HC - 1 else "mgu")
                        mgu_done[(tb, oc, tt, "g")] = cnt["mgu"]
                        for h in range(HC):
                            MM(psu[pp][:, :], wub[par][:, h * 128:(h + 1) * 128],
                               xnb[h][:, tt * 512:(tt + 1) * 512], h == 0, h == HC - 1,
                               sem=None if h < HC - 1 else "mgu")
                        mgu_done[(tb, oc, tt, "u")] = cnt["mgu"]
                        W("scalar", "mgu", mgu_done[(tb, oc, tt, "g")])
                        if use >= 2:
                            W("scalar", "xin", xin_done[use - 2])
                        OP("scalar", lambda e, o=g_t[pp][:, :], i=psg[pp][:, :]:
                           e.activation(o, i, act_fn))
                        W("scalar", "mgu", mgu_done[(tb, oc, tt, "u")])
                        egu_done[use] = OP("scalar",
                                           lambda e, o=u_t[pp][:, :], i=psu[pp][:, :]:
                                           e.copy(o, i), sem="egu")
                        W("vector", "egu", egu_done[use])
                        if ol == 0 and tt == 0 and (tb or og):
                            prev = md_done[(tb, og - 1, 15, NTT - 1)] if og else \
                                md_done[(tb - 1, NOG - 1, 15, NTT - 1)]
                            W("vector", "md", prev)
                        xin_done[use] = OP(
                            "vector",
                            lambda e, o=inter[ol][:, tt * 512:(tt + 1) * 512],
                            a=g_t[pp][:, :], b=u_t[pp][:, :]:
                            e.tensor_tensor(o, a, b, ALU.mult), sem="xin")
                for h2 in range(HC):
                    j = (tb * NOG + og) * HC + h2
                    par = j % 2
                    if j >= 2:
                        W("sync", "md", md_done_j(md_done, j - 1, NOG, HC, NTT))
                    ldd = DMA("sync", wdb[par][:], wd_s[og, h2])
                    W("tensor", "ld", ldd)
                    W("tensor", "xin", xin_done[(tb * OC + og * G + G - 1) * NTT + NTT - 1])
                    for tt in range(NTT):
                        pd = j * NTT + tt
                        pp = pd % 2
                        if pd >= 2:
                            W("tensor", "aup", aup_done[pd - 2])
                        for ol in range(G):
                            MM(psd[pp][:, :], wdb[par][:, ol * 128:(ol + 1) * 128],
                               inter[ol][:, tt * 512:(tt + 1) * 512], ol == 0,
                               ol == G - 1, sem=None if ol < G - 1 else "md")
                        md_done[(tb, og, h2, tt)] = cnt["md"]
                        W("vector", "md", md_done[(tb, og, h2, tt)])
                        ssl = slice(tt * 512, (tt + 1) * 512)
                        if multi:
                            if og == 0 and tb > 0 and tt == 0:
                                W("vector", "st", st_done[(tb - 1, h2)])
                            if og == 0:
                                aup_done[pd] = OP(
                                    "vector", lambda e, o=accb[h2][:, ssl],
                                    i=psd[pp][:, :]: e.tensor_copy(o, i), sem="aup")
                            else:
                                aup_done[pd] = OP(
                                    "vector", lambda e, o=accb[h2][:, ssl],
                                    i=psd[pp][:, :]:
                                    e.tensor_tensor(o, o, i, ALU.add), sem="aup")
                        else:
                            ob = outb[h2 % 2]
                            if h2 >= 2 and tt == 0:
                                W("vector", "st", st_done[(tb, h2 - 2)])
                            elif h2 < 2 and tb > 0 and tt == 0:
                                W("vector", "st", st_done[(tb - 1, h2 + HC - 2)])
                            aup_done[pd] = OP(
                                "vector", lambda e, o=ob[:, ssl],
                                i=psd[pp][:, :]: e.tensor_copy(o, i), sem="aup")
                    if (not multi) or og == NOG - 1:
                        W("gpsimd", "aup", aup_done[j * NTT + NTT - 1])
                        src = accb[h2] if multi else outb[h2 % 2]
                        st_done[(tb, h2)] = DMA(
                            "gpsimd", out[h2 * 128:(h2 + 1) * 128, tsl],
                            src[:, :], sem="st")

        @block.sync
        def _(sy):
            for f in sched["sync"]:
                f(sy)
            sy.wait_ge(semh["st"], cnt["st"])

        @block.tensor
        def _(t):
            for f in sched["tensor"]:
                f(t)

        @block.scalar
        def _(se):
            for f in sched["scalar"]:
                f(se)

        @block.vector
        def _(v):
            for f in sched["vector"]:
                f(v)

        @block.gpsimd
        def _(g):
            for f in sched["gpsimd"]:
                f(g)
    return nc


def md_done_j(md_done, j, NOG, HC, NTT):
    for key, vval in reversed(md_done.items()):
        tb, og, h2, tt = key
        if (tb * NOG + og) * HC + h2 == j and tt == NTT - 1:
            return vval
    return 0


def ffn_streams(WgT, WuT, WdT, OC, G):
    HC = D // 128
    NOG = OC // G

    def gu(WT):
        return np.ascontiguousarray(
            WT.reshape(HC, 128, OC, 128).transpose(2, 1, 0, 3).reshape(OC, 128, D))

    wd = np.ascontiguousarray(
        WdT.reshape(NOG, G, 128, HC, 128).transpose(0, 3, 2, 1, 4)
        .reshape(NOG, HC, 128, G * 128))
    return gu(WgT), gu(WuT), wd


def run_ffn(xn_list, WgT, WuT, WdT, OC, G, TB, act_fn):
    wg_s, wu_s, wd_s = ffn_streams(WgT, WuT, WdT, OC, G)
    nc = build_ffn_nc(OC, G, TB, act_fn)
    in_maps = [{"xn": xn_list[c], "wg_s": wg_s, "wu_s": wu_s, "wd_s": wd_s}
               for c in range(8)]
    res = run_bass_kernel_spmd(nc, in_maps, core_ids=list(range(8)))
    return [res.results[c]["out"] for c in range(8)]



def build_mm_nc(Kd, Md, has_bias, out_f32, w_f32):
    KC = Kd // 128
    MC = Md // 128
    NTT = SQ // 512
    nc = bass.Bass("TRN2", target_bir_lowering=False)
    x = nc.dram_tensor("x", [Kd, SQ], BF16, kind="ExternalInput").ap()
    w_s = nc.dram_tensor("w_s", [MC, 128, Kd], F32 if w_f32 else BF16,
                         kind="ExternalInput").ap()
    bias = nc.dram_tensor("bias", [128, MC], F32, kind="ExternalInput").ap() \
        if has_bias else None
    out = nc.dram_tensor("out", [Md, SQ], F32 if out_f32 else BF16,
                         kind="ExternalOutput").ap()

    sched = {k: [] for k in ["sync", "tensor", "scalar", "vector", "gpsimd"]}
    cnt = {}

    with ExitStack() as ctx:
        def sb(name, shape, dt):
            return ctx.enter_context(nc.sbuf_tensor(name, shape, dt))
        xb = [sb(f"xb{h}", [128, 512], BF16) for h in range(KC)]
        wb = [sb(f"wb{i}", [128, Kd], BF16) for i in range(2)]
        ob = [sb(f"ob{i}", [128, 512], F32 if out_f32 else BF16) for i in range(2)]
        bb = sb("bb", [128, MC], F32) if has_bias else None
        ps = [ctx.enter_context(nc.psum_tensor(f"ps{i}", [128, 512], F32))
              for i in range(2)]
        semh = {n: ctx.enter_context(nc.semaphore(n))
                for n in ["ld", "lw", "mm", "ev", "st"]}
        block = ctx.enter_context(nc.Block())

        def W(eng, sem, val):
            if val > 0:
                sched[eng].append(lambda e, s=semh[sem], v=val: e.wait_ge(s, v))

        def bump(sem, n=1):
            cnt[sem] = cnt.get(sem, 0) + n
            return cnt[sem]

        def DMA(eng, dst, src, sem="ld"):
            v = bump(sem, 16)
            sched[eng].append(
                lambda e, d=dst, sr=src, sm=semh[sem]:
                e.dma_start(out=d, in_=sr).then_inc(sm, 16))
            return v

        def MM(out_ap, l_ap, r_ap, st_, sp_, sem=None):
            v = bump(sem) if sem else None
            sm = semh[sem] if sem else None
            sched["tensor"].append(
                lambda e, o=out_ap, l=l_ap, r=r_ap, a=st_, b=sp_, s=sm:
                (e.matmul(o, l, r, start=a, stop=b).then_inc(s, 1) if s else
                 e.matmul(o, l, r, start=a, stop=b)))
            return v

        def OP(eng, fn, sem=None):
            v = bump(sem) if sem else None
            sm = semh[sem] if sem else None
            sched[eng].append(
                lambda e, f=fn, s=sm: (f(e).then_inc(s, 1) if s else f(e)))
            return v

        if has_bias:
            DMA("sync", bb[:, :], bias)
        mm_done = {}
        ev_done = {}
        st_done = {}
        weng = "gpsimd" if w_f32 else "sync"
        for tt in range(NTT):
            tsl = slice(tt * 512, (tt + 1) * 512)
            if tt:
                W("sync", "mm", mm_done[(tt - 1, MC - 1)])
            for h in range(KC):
                ld_x = DMA("sync", xb[h][:, :], x[h * 128:(h + 1) * 128, tsl])
            for mc in range(MC):
                u = tt * MC + mc
                if u >= 2:
                    W(weng, "mm", mm_done[(tt, mc - 1) if mc >= 2 else
                                          (tt - 1, MC - 1)])
                lw = DMA(weng, wb[u % 2][:, :], w_s[mc], sem="lw")
                W("tensor", "lw", lw)
                W("tensor", "ld", ld_x)
                if u >= 2:
                    W("tensor", "ev", ev_done[u - 2])
                for h in range(KC):
                    MM(ps[u % 2][:, :], wb[u % 2][:, h * 128:(h + 1) * 128],
                       xb[h][:, :], h == 0, h == KC - 1,
                       sem=None if h < KC - 1 else "mm")
                mm_done[(tt, mc)] = cnt["mm"]
                W("scalar", "mm", mm_done[(tt, mc)])
                if u >= 2:
                    W("scalar", "st", st_done[u - 2])
                if has_bias:
                    ev_done[u] = OP("scalar",
                                    lambda e, o=ob[u % 2][:, :], i=ps[u % 2][:, :],
                                    bc=bb[:, mc:mc + 1]:
                                    e.activation(o, i, AF.Identity, bias=bc), sem="ev")
                else:
                    ev_done[u] = OP("scalar",
                                    lambda e, o=ob[u % 2][:, :], i=ps[u % 2][:, :]:
                                    e.copy(o, i), sem="ev")
                W("gpsimd" if not w_f32 else "sync", "ev", ev_done[u])
                st_done[u] = DMA("gpsimd" if not w_f32 else "sync",
                                 out[mc * 128:(mc + 1) * 128, tsl],
                                 ob[u % 2][:, :], sem="st")

        @block.sync
        def _(sy):
            for f in sched["sync"]:
                f(sy)
            sy.wait_ge(semh["st"], cnt["st"])

        @block.tensor
        def _(t):
            for f in sched["tensor"]:
                f(t)

        @block.scalar
        def _(se):
            for f in sched["scalar"]:
                f(se)

        @block.vector
        def _(v):
            for f in sched["vector"]:
                f(v)

        @block.gpsimd
        def _(g):
            for f in sched["gpsimd"]:
                f(g)
    return nc


def mm_stream(WT):
    Kd, Md = WT.shape
    KC, MC = Kd // 128, Md // 128
    return np.ascontiguousarray(
        WT.reshape(KC, 128, MC, 128).transpose(2, 1, 0, 3).reshape(MC, 128, Kd))


def run_mm(x_list, WT, has_bias=False, bias=None, out_f32=False):
    Kd, Md = WT.shape
    w_f32 = WT.dtype == np.float32
    nc = build_mm_nc(Kd, Md, has_bias, out_f32, w_f32)
    w_s = mm_stream(WT)
    in_maps = []
    for c in range(8):
        m = {"x": x_list[c], "w_s": w_s}
        if has_bias:
            m["bias"] = np.ascontiguousarray(bias.reshape(Md // 128, 128).T)
        in_maps.append(m)
    res = run_bass_kernel_spmd(nc, in_maps, core_ids=list(range(8)))
    return [res.results[c]["out"] for c in range(8)]



ATTN_SCALE = 0.08838834764831845
ATTN_DEBUG = False


def build_attn_nc(causal, groups):
    HEADS = 16
    HPG = HEADS // groups
    NQT = 16
    nc = bass.Bass("TRN2", target_bir_lowering=False)
    q = nc.dram_tensor("q", [2048, SQ], BF16, kind="ExternalInput").ap()
    k = nc.dram_tensor("k", [groups * 128, SQ], BF16, kind="ExternalInput").ap()
    v = nc.dram_tensor("v", [groups * 128, SQ], BF16, kind="ExternalInput").ap()
    out = nc.dram_tensor("out", [2048, SQ], BF16, kind="ExternalOutput").ap()
    if ATTN_DEBUG:
        dbg_s = nc.dram_tensor("dbg_s", [4, 128, 512], F32, kind="ExternalOutput").ap()
        dbg_p = nc.dram_tensor("dbg_p", [4, 128, 512], BF16, kind="ExternalOutput").ap()
        dbg_r = nc.dram_tensor("dbg_r", [4, 128, 2], F32, kind="ExternalOutput").ap()

    sched = {kk: [] for kk in ["sync", "tensor", "scalar", "vector", "gpsimd"]}
    cnt = {}

    with ExitStack() as ctx:
        def sb(name, shape, dt):
            return ctx.enter_context(nc.sbuf_tensor(name, shape, dt))
        qh = [sb(f"qh{i}", [128, SQ], BF16) for i in range(2)]
        kg = [sb(f"kg{i}", [128, SQ], BF16) for i in range(2)]
        vg = [sb(f"vg{i}", [128, SQ], BF16) for i in range(2)]
        vtok = [sb(f"vtok{i}", [128, SQ], BF16) for i in range(2)]
        s_row = sb("s_row", [128, SQ], F32)
        p_t = sb("p_t", [128, SQ], BF16)
        pt_all = sb("pt_all", [128, 16 * 512], BF16)
        ctxb = [sb(f"ctxb{i}", [128, 512], BF16) for i in range(2)]
        maxc = sb("maxc", [128, 1], F32)
        negm = sb("negm", [128, 1], F32)
        sumc = sb("sumc", [128, 1], F32)
        rsum = sb("rsum", [128, 1], F32)
        identb = sb("identb", [128, 128], BF16)
        identi = sb("identi", [128, 128], I32)
        identf = sb("identf", [128, 128], F32)
        masks = sb("masks", [128, 4 * 512], F32) if causal else None
        ps_s = [ctx.enter_context(nc.psum_tensor(f"ps_s{i}", [128, 512], F32))
                for i in range(2)]
        ps_t = [ctx.enter_context(nc.psum_tensor(f"ps_t{i}", [128, 1024], BF16))
                for i in range(2)]
        ps_c = ctx.enter_context(nc.psum_tensor("ps_c", [128, 512], F32))
        semh = {n: ctx.enter_context(nc.semaphore(n))
                for n in ["ld", "setup", "ms", "es", "xm", "ep", "pn", "mt",
                          "ptc", "mc", "ec", "st", "vtk", "dbs", "dbv"]}
        dbg_rt = sb("dbg_rt", [128, 2], F32) if ATTN_DEBUG else None
        block = ctx.enter_context(nc.Block())

        def W(eng, sem, val):
            if val > 0:
                sched[eng].append(lambda e, s=semh[sem], v=val: e.wait_ge(s, v))

        def bump(sem, n=1):
            cnt[sem] = cnt.get(sem, 0) + n
            return cnt[sem]

        def DMA(eng, dst, src, sem="ld"):
            vv = bump(sem, 16)
            sched[eng].append(
                lambda e, d=dst, sr=src, sm=semh[sem]:
                e.dma_start(out=d, in_=sr).then_inc(sm, 16))
            return vv

        def MM(out_ap, l_ap, r_ap, st_, sp_, sem=None):
            vv = bump(sem) if sem else None
            sm = semh[sem] if sem else None
            sched["tensor"].append(
                lambda e, o=out_ap, l=l_ap, r=r_ap, a=st_, b=sp_, s=sm:
                (e.matmul(o, l, r, start=a, stop=b).then_inc(s, 1) if s else
                 e.matmul(o, l, r, start=a, stop=b)))
            return vv

        def TR(out_ap, in_ap, sem=None):
            vv = bump(sem) if sem else None
            sm = semh[sem] if sem else None
            sched["tensor"].append(
                lambda e, o=out_ap, i=in_ap, s=sm:
                (e.transpose(o, i, identb[:, :]).then_inc(s, 1) if s else
                 e.transpose(o, i, identb[:, :])))
            return vv

        def OP(eng, fn, sem=None):
            vv = bump(sem) if sem else None
            sm = semh[sem] if sem else None
            sched[eng].append(
                lambda e, f=fn, s=sm: (f(e).then_inc(s, 1) if s else f(e)))
            return vv

        OP("gpsimd", lambda e: e.iota(identi[:, :], pattern=[[1, 128]], base=0,
                                      channel_multiplier=-1), sem="setup")
        W("vector", "setup", 1)
        OP("vector", lambda e: e.tensor_scalar(identf[:, :], identi[:, :], 0.0,
                                               None, ALU.is_equal))
        OP("vector", lambda e: e.tensor_copy(identb[:, :], identf[:, :]),
           sem="setup")
        if causal:
            for mi in range(4):
                OP("vector", lambda e, m=masks[:, mi * 512:(mi + 1) * 512]:
                   e.memset(m, 0.0), sem="setup")
            W("gpsimd", "setup", 2 + 4)
            for mi in range(4):
                OP("gpsimd", lambda e, m=masks[:, mi * 512:(mi + 1) * 512], off=mi * 128:
                   e.affine_select(m, m, pattern=[[-1, 512]], compare_op=ALU.is_ge,
                                   fill=NEG, base=off, channel_multiplier=1),
                   sem="setup")
        SETUP = cnt["setup"]

        ms_done = {}
        es_done = {}
        pn_done = {}
        mt_done = {}
        ptc_done = {}
        mc_done = {}
        ec_done = {}
        st_done = {}
        vtk_done = {}

        prev_p_mt = 0
        su = 0
        tu = 0
        cu = 0

        for h in range(HEADS):
            g = h // HPG
            par = h % 2
            if h >= 2:
                W("sync", "ms", ms_done[h - 2])
                W("sync", "mc", mc_done[h - 2])
            DMA("sync", qh[par][:, :], q[h * 128:(h + 1) * 128, :])
            DMA("sync", kg[par][:, :], k[g * 128:(g + 1) * 128, :])
            ld_h = DMA("sync", vg[par][:, :], v[g * 128:(g + 1) * 128, :])
            W("tensor", "ld", ld_h)
            if h == 0:
                W("tensor", "setup", SETUP)
            if h >= 2:
                W("vector", "mc", mc_done[h - 2])
            for kt in range(16):
                if tu >= 2:
                    W("tensor", "ptc", ptc_done[tu - 2])
                TR(ps_t[tu % 2][:, 0:128], vg[par][:, kt * 128:(kt + 1) * 128],
                   sem="mt")
                mt_done[tu] = cnt["mt"]
                W("vector", "mt", mt_done[tu])
                ptc_done[tu] = OP("vector",
                                  lambda e, o=vtok[par][:, kt * 128:(kt + 1) * 128],
                                  i=ps_t[tu % 2][:, 0:128]:
                                  e.tensor_copy(o, i), sem="ptc")
                vtk_done[h] = ptc_done[tu]
                tu += 1
            for qg in range(4):
                KC = (qg + 1) if causal else 4
                KT = 4 * KC
                for qi in range(4):
                    qt = qg * 4 + qi
                    ev_eng = "vector" if causal else "scalar"
                    for kc in range(KC):
                        if su >= 2:
                            W("tensor", "es", es_done[su - 2])
                        MM(ps_s[su % 2][:, :], qh[par][:, qt * 128:(qt + 1) * 128],
                           kg[par][:, kc * 512:(kc + 1) * 512], True, True, sem="ms")
                        ms_done[h] = cnt["ms"]
                        diag = causal and kc == KC - 1
                        ssl = slice(kc * 512, (kc + 1) * 512)
                        W(ev_eng, "ms", cnt["ms"])
                        if diag:
                            es_done[su] = OP(
                                ev_eng,
                                lambda e, o=s_row[:, ssl], i=ps_s[su % 2][:, :],
                                m=masks[:, (qt % 4) * 512:(qt % 4 + 1) * 512]:
                                e.scalar_tensor_tensor(o, i, ATTN_SCALE, m,
                                                       ALU.mult, ALU.add),
                                sem="es")
                        elif causal:
                            es_done[su] = OP(
                                ev_eng,
                                lambda e, o=s_row[:, ssl], i=ps_s[su % 2][:, :]:
                                e.tensor_scalar(o, i, ATTN_SCALE, None, ALU.mult),
                                sem="es")
                        else:
                            es_done[su] = OP(
                                ev_eng,
                                lambda e, o=s_row[:, ssl], i=ps_s[su % 2][:, :]:
                                e.activation(o, i, AF.Identity, scale=ATTN_SCALE),
                                sem="es")
                        su += 1
                    kext = KC * 512
                    W("vector", "es", es_done[su - 1])
                    OP("vector", lambda e, i=s_row[:, 0:kext]:
                       e.reduce_max(negm[:, :], i, axis=AX.X, negate=True),
                       sem="xm")
                    xm_v = cnt["xm"]
                    W("scalar", "xm", xm_v)
                    if prev_p_mt:
                        W("scalar", "mt", prev_p_mt)
                    ep_done = OP("scalar",
                                 lambda e, o=p_t[:, 0:kext], i=s_row[:, 0:kext]:
                                 e.activation(o, i, AF.Exp, bias=negm[:, 0:1],
                                              accum_out=sumc[:, :]), sem="ep")
                    W("vector", "ep", ep_done)
                    OP("vector", lambda e: e.reciprocal(rsum[:, :], sumc[:, :]))
                    OP("vector", lambda e: e.tensor_copy(maxc[:, :], rsum[:, :]))
                    OP("vector", lambda e: e.tensor_copy(maxc[:, :], rsum[:, :]))
                    pn_done[qt] = OP("vector",
                                     lambda e, o=p_t[:, 0:kext], kx=kext:
                                     e.tensor_tensor(o, o,
                                                     rsum[:, 0:1].broadcast_to((128, kx)),
                                                     ALU.mult), sem="pn")
                    if ATTN_DEBUG and h == 0 and qg == 0:
                        W("gpsimd", "pn", pn_done[qt])
                        DMA("gpsimd", dbg_s[qt], s_row[:, 0:512], sem="dbs")
                        DMA("gpsimd", dbg_p[qt], p_t[:, 0:512], sem="dbs")
                        OP("vector", lambda e, o=dbg_rt[:, 0:1]:
                           e.tensor_copy(o, sumc[:, :]))
                        OP("vector", lambda e, o=dbg_rt[:, 1:2]:
                           e.tensor_copy(o, rsum[:, :]), sem="dbv")
                        W("gpsimd", "dbv", cnt["dbv"])
                        DMA("gpsimd", dbg_r[qt], dbg_rt[:, :], sem="dbs")
                        W("vector", "dbs", cnt["dbs"])
                    W("tensor", "pn", pn_done[qt])
                    for kt in range(KT):
                        if tu >= 2:
                            W("tensor", "ptc", ptc_done[tu - 2])
                        TR(ps_t[tu % 2][:, 0:128], p_t[:, kt * 128:(kt + 1) * 128],
                           sem="mt")
                        mt_done[tu] = cnt["mt"]
                        W("vector", "mt", mt_done[tu])
                        dst = pt_all[:, kt * 512 + qi * 128: kt * 512 + (qi + 1) * 128]
                        ptc_done[tu] = OP("vector",
                                          lambda e, o=dst, i=ps_t[tu % 2][:, 0:128]:
                                          e.tensor_copy(o, i), sem="ptc")
                        tu += 1
                    prev_p_mt = cnt["mt"]
                W("tensor", "ptc", ptc_done[tu - 1])
                if cu >= 1:
                    W("tensor", "ec", ec_done[cu - 1])
                for kt in range(KT):
                    MM(ps_c[:, :], vtok[par][:, kt * 128:(kt + 1) * 128],
                       pt_all[:, kt * 512:(kt + 1) * 512], kt == 0, kt == KT - 1,
                       sem=None if kt < KT - 1 else "mc")
                mc_done[h] = cnt["mc"]
                W("scalar", "mc", cnt["mc"])
                if cu >= 2:
                    W("scalar", "st", st_done[cu - 2])
                ec_done[cu] = OP("scalar",
                                 lambda e, o=ctxb[cu % 2][:, :], i=ps_c[:, :]:
                                 e.copy(o, i), sem="ec")
                W("gpsimd", "ec", ec_done[cu])
                st_done[cu] = DMA("gpsimd",
                                  out[h * 128:(h + 1) * 128,
                                      qg * 512:(qg + 1) * 512],
                                  ctxb[cu % 2][:, :], sem="st")
                cu += 1

        @block.sync
        def _(sy):
            for f in sched["sync"]:
                f(sy)
            sy.wait_ge(semh["st"], cnt["st"])

        @block.tensor
        def _(t):
            for f in sched["tensor"]:
                f(t)

        @block.scalar
        def _(se):
            for f in sched["scalar"]:
                f(se)

        @block.vector
        def _(v_):
            for f in sched["vector"]:
                f(v_)

        @block.gpsimd
        def _(g_):
            for f in sched["gpsimd"]:
                f(g_)
    return nc


def run_attn(q_list, k_list, v_list, causal, groups):
    nc = build_attn_nc(causal, groups)
    in_maps = [{"q": q_list[c], "k": k_list[c], "v": v_list[c]} for c in range(8)]
    res = run_bass_kernel_spmd(nc, in_maps, core_ids=list(range(8)))
    return [res.results[c]["out"] for c in range(8)]



def _c(a):
    return np.ascontiguousarray(a)


def kernel(**inputs):
    deq = run_dequant(inputs)

    x = inputs["L0_attn_x"]
    xs = [_c(x[c].T) for c in range(8)]

    _, xn = run_resid_ln(xs, [], out_x=False, out_xn=True)

    q0 = run_mm(xn, _c(inputs["L0_attn_w_q"]))
    kv0 = run_mm(xn, _c(inputs["L0_attn_w_kv"]))
    k0 = [_c(kv[0:512]) for kv in kv0]
    v0 = [_c(kv[512:1024]) for kv in kv0]
    ctx0 = run_attn(q0, k0, v0, causal=True, groups=4)
    a0 = run_mm(ctx0, _c(inputs["L0_attn_w_dense"]), out_f32=True)

    f0 = run_ffn(xn, _c(deq["wgg"].T), _c(deq["wgu"].T), _c(deq["wgd"].T),
                 OC=GINTER // 128, G=GINTER // 128, TB=1024, act_fn=AF.Silu)

    x1, xn2 = run_resid_ln(xs, [a0, f0], out_x=True, out_xn=True)

    qkv1 = run_mm(xn2, _c(inputs["L1_attn_Wqkv"].T))
    q1 = [_c(np.concatenate([qk[384 * n:384 * n + 128] for n in range(16)], 0))
          for qk in qkv1]
    k1 = [_c(np.concatenate([qk[384 * n + 128:384 * n + 256] for n in range(16)], 0))
          for qk in qkv1]
    v1 = [_c(np.concatenate([qk[384 * n + 256:384 * n + 384] for n in range(16)], 0))
          for qk in qkv1]
    ctx1 = run_attn(q1, k1, v1, causal=False, groups=16)
    ctx_scr = []
    for b_new in range(8):
        rows = []
        for nh_new in range(16):
            g = 8 * nh_new + b_new
            b_idx, nh_idx = g // 16, g % 16
            rows.append(ctx1[b_idx][nh_idx * 128:(nh_idx + 1) * 128])
        ctx_scr.append(_c(np.concatenate(rows, 0)))
    a1 = run_mm(ctx_scr, _c(inputs["L1_attn_Wproj"].T), has_bias=True,
                bias=_c(inputs["L1_attn_bproj"]), out_f32=True)

    f1 = run_ffn(xn2, _c(deq["wqg"].T), _c(deq["wqu"].T), _c(deq["wqd"].T),
                 OC=QINTER // 128, G=16, TB=1024, act_fn=AF.Gelu_apprx_tanh)

    x2, _ = run_resid_ln(x1, [a1, f1], out_x=True, out_xn=False)

    return np.stack([x2[c].T for c in range(8)], 0)
